# version 02
# baseline (speedup 1.0000x reference)
import numpy as np
import concourse.bass as bass
import concourse.mybir as mybir
from concourse.bass_utils import run_bass_kernel_spmd
from contextlib import ExitStack

F32 = mybir.dt.float32
F16 = mybir.dt.float16
AF = mybir.ActivationFunctionType
ALU = mybir.AluOpType

B_TOT, T_SEQ, D_MODEL = 16, 2048, 1024
D_INNER, D_STATE, D_CONV, DT_RANK = 2048, 16, 4, 64
N_CORES = 8
B_PER = B_TOT // N_CORES
TC = 512
N_TC = T_SEQ // TC
KD = D_MODEL // 128
KI = D_INNER // 128
CLAMP_MAX = float(np.log(1.0 / 0.01))
LN_EPS = 1e-5
DBG = False
DBG_UNIT = (0, 0)
SERIAL = False
SERIAL_PHASES = set()
SERIAL_UNIT = None

ENGINES = ("T", "V", "S", "G", "Y")
N_DMA_LANES = 32
DMA_LANES = tuple(f"Y{i}" for i in range(N_DMA_LANES))


class Tok:
    __slots__ = ("eng", "cnt", "clock", "dma", "slow")

    def __init__(self, eng, cnt, clock, dma, slow=False):
        self.eng, self.cnt, self.clock, self.dma = eng, cnt, clock, dma
        self.slow = slow


class Buf:
    __slots__ = ("name", "writer", "readers")

    def __init__(self, name):
        self.name, self.writer, self.readers = name, None, []


class Plan:
    def __init__(self):
        self.ops = {e: [] for e in ENGINES}
        self.cnt = {e: 0 for e in ENGINES}
        for lane in DMA_LANES:
            self.cnt[lane] = 0
        self.clock = {e: {} for e in ENGINES}
        self.n_waits = 0
        self.n_ops = 0
        self.ydma = 0

    def op(self, eng, fn, reads=(), writes=(), dma=False, label="",
           sync_self=False, slow_write=False, fence_fn=None):
        if dma:
            lane = DMA_LANES[self.ydma % N_DMA_LANES]
            self.ydma += 1
        else:
            lane = eng
        waits = {}
        clk = self.clock[eng]

        def need(tok):
            if tok is None:
                return
            if tok.eng == eng:
                if tok.dma or tok.slow or sync_self:
                    if waits.get(eng, 0) < tok.cnt:
                        waits[eng] = tok.cnt
                return
            if clk.get(tok.eng, 0) >= tok.cnt:
                return
            if waits.get(tok.eng, 0) < tok.cnt:
                waits[tok.eng] = tok.cnt
            for e2, v in tok.clock.items():
                if clk.get(e2, 0) < v:
                    clk[e2] = v

        for b in reads:
            need(b.writer)
        for b in writes:
            need(b.writer)
            for r in b.readers:
                need(r)
        if SERIAL or getattr(self, "phase", None) in SERIAL_PHASES:
            for e2 in list(self.cnt):
                if e2 != eng and self.cnt[e2] > 0 and                    waits.get(e2, 0) < self.cnt[e2]:
                    waits[e2] = self.cnt[e2]
        inc = 16 if dma else 1
        self.cnt[lane] += inc
        if dma:
            tok_clock = dict(clk)
            tok_clock[lane] = self.cnt[lane]
        else:
            clk[lane] = self.cnt[lane]
            tok_clock = dict(clk)
        tok = Tok(lane, self.cnt[lane], tok_clock, dma, slow_write)
        self.ops[eng].append((fn, waits, inc, label, lane))
        self.n_waits += len(waits)
        self.n_ops += 1
        if fence_fn is not None:
            self.cnt[eng] += 1
            clk[eng] = self.cnt[eng]
            tok = Tok(eng, self.cnt[eng], dict(clk), False, slow_write)
            self.ops[eng].append((fence_fn, {}, 1, label + "_fence", eng))
            self.n_ops += 1
        for b in reads:
            b.readers.append(tok)
        for b in writes:
            b.writer = tok
            b.readers = []
        return tok


def build_nc(n_units=None):
    nc = bass.Bass("TRN2", target_bir_lowering=False)

    def din(name, shape, dtype=F32):
        return nc.dram_tensor(name, shape, dtype, kind="ExternalInput").ap()

    xT = din("xT", [B_PER, D_MODEL, T_SEQ])
    w_inproj = din("w_inproj", [D_MODEL, 2 * D_INNER])
    w_xproj = din("w_xproj", [D_INNER, 96])
    w_dtproj = din("w_dtproj", [DT_RANK, D_INNER])
    w_outproj = din("w_outproj", [D_INNER, D_MODEL])
    w_fg = din("w_fg", [D_MODEL, D_MODEL])
    wT_ffn = din("wT_ffn", [D_MODEL, D_MODEL])
    wT_wp = din("wT_wp", [D_MODEL, 256])
    wT_wd = din("wT_wd", [D_MODEL, 32])
    wT_wu = din("wT_wu", [4, 8, D_MODEL])
    conv_w2 = din("conv_w2", [128, KI * D_CONV])
    A_logt = din("A_logt", [128, KI * D_STATE])
    dt_bias = din("dt_bias", [128, KI])
    conv_bias = din("conv_bias", [128, KI])
    D_vec = din("D_vec", [128, KI])
    fg_bias = din("fg_bias", [128, KD])
    bp_bias = din("bp_bias", [128, 2])
    sim_in = din("sim_in", [256, 4])
    temp_in = din("temp_in", [1, 1])
    ident_in = din("ident_in", [128, 128])

    outT = nc.dram_tensor("outT", [B_PER, D_MODEL, T_SEQ], F32,
                          kind="ExternalOutput").ap()
    dbg16 = (nc.dram_tensor("dbg16", [24, 128, 512], F16,
                            kind="ExternalOutput").ap() if DBG else None)
    dbg32 = (nc.dram_tensor("dbg32", [12, 128, 512], F32,
                            kind="ExternalOutput").ap() if DBG else None)
    s_inproj = nc.dram_tensor("s_inproj", [D_MODEL, 2 * D_INNER], F16,
                              kind="Internal").ap()
    s_outproj = nc.dram_tensor("s_outproj", [D_INNER, D_MODEL], F16,
                               kind="Internal").ap()
    s_fg = nc.dram_tensor("s_fg", [D_MODEL, D_MODEL], F16, kind="Internal").ap()
    s_ffn = nc.dram_tensor("s_ffn", [D_MODEL, D_MODEL], F16, kind="Internal").ap()
    s_bcrows = nc.dram_tensor("s_bcrows", [32, TC], F16, kind="Internal").ap()
    s_retf = nc.dram_tensor("s_retf", [4, TC], F16, kind="Internal").ap()

    ctx = ExitStack()
    sb_ = lambda n, s, d: ctx.enter_context(nc.sbuf_tensor(n, s, d))
    sems = {e: ctx.enter_context(nc.semaphore("sem" + e)) for e in ENGINES}
    for lane in DMA_LANES:
        sems[lane] = ctx.enter_context(nc.semaphore("sem" + lane))

    stream = [sb_(f"stream{i}", [128, 4096], F16) for i in range(2)]
    cvt16 = sb_("cvt16", [128, 4096], F16)
    new_all = cvt16
    x32 = [sb_(f"x32_{i}", [128, 512], F32) for i in range(2)]
    x_h = sb_("x_h", [128, KD, 512], F16)
    xi_pad = sb_("xi_pad", [128, KI, 515], F16)
    u_all = sb_("u_all", [128, KI, 512], F16)
    zsil = sb_("zsil", [128, KI, 512], F16)
    bc_B = sb_("bc_B", [128, 16, 512], F16)
    bc_C = sb_("bc_C", [128, 16, 512], F16)
    cvt32 = bc_B[:].rearrange("p k c -> p (k c)").bitcast(F32)[:, 0:2048]
    dt_t = [sb_(f"dt_t{i}", [128, 512], F16) for i in range(2)]
    sp1 = sb_("sp1", [128, 512], F16)
    sp2 = sb_("sp2", [128, 512], F16)
    w_t = [sb_(f"w_t{i}", [128, 512], F16) for i in range(2)]
    dA_t = [sb_(f"dA_t{i}", [128, 512], F16) for i in range(2)]
    q_t = [sb_(f"q_t{i}", [128, 512], F16) for i in range(2)]
    h_t = [sb_(f"h_t{i}", [128, 512], F16) for i in range(2)]
    g_t = [sb_(f"g_t{i}", [128, 512], F16) for i in range(2)]
    y_f = [sb_(f"y_f{i}", [128, 512], F16) for i in range(2)]
    y_g = xi_pad[:][:, :, 0:512]
    x2_32 = sb_("x2_32", [128, KD, 512], F32)
    x2_h = sb_("x2_h", [128, KD, 512], F16)
    s_f = [sb_(f"s_f{i}", [128, 512], F32) for i in range(2)]
    sq_f = [sb_(f"sq_f{i}", [128, 512], F32) for i in range(2)]
    x3_f = s_f
    wq_xproj = sb_("wq_xproj", [128, KI * 96], F16)
    wq_dtproj = sb_("wq_dtproj", [64, KI * 128], F16)
    wq_wp = sb_("wq_wp", [128, KD * 256], F16)
    wq_wd = sb_("wq_wd", [128, KD * 32], F16)
    wq_wu = [sb_(f"wq_wu{i}", [8, D_MODEL], F16) for i in range(4)]
    wq_conv = sb_("wq_conv", [128, KI * D_CONV], F32)
    A_t = sb_("A_t", [128, KI * D_STATE], F32)
    A_pos = sb_("A_pos", [128, KI * D_STATE], F32)
    b_dt = sb_("b_dt", [128, KI], F32)
    b_conv = sb_("b_conv", [128, KI], F32)
    b_D = sb_("b_D", [128, KI], F32)
    b_fg = sb_("b_fg", [128, KD], F32)
    b_bp = sb_("b_bp", [128, 2], F32)
    sim_h = sb_("sim_h", [128, 8], F16)
    sc4 = sb_("sc4", [4, 1], F32)
    ident = sb_("ident", [128, 128], F32)
    ident_h = sb_("ident_h", [128, 128], F16)
    ones_r32 = sb_("ones_r32", [1, 128], F32)
    ones_c32 = sb_("ones_c32", [128, 1], F32)
    states = [sb_(f"states{b}", [128, KI * D_STATE], F32) for b in range(B_PER)]
    hist = [sb_(f"hist{b}", [128, KI, 3], F16) for b in range(B_PER)]
    dtlow = sb_("dtlow", [64, 512], F16)
    bcrows = sb_("bcrows", [32, 512], F16)
    row_m = sb_("row_m", [1, 512], F32)
    row_v = sb_("row_v", [1, 512], F32)
    row_t = sb_("row_t", [1, 512], F32)
    row_r = sb_("row_r", [1, 512], F32)
    lt4 = sq_f[0][0:4, :]
    ltt = sb_("ltt", [128, 20], F32)
    retc = sb_("retc", [128, 16], F32)
    retf = sb_("retf", [4, 512], F16)
    rbf = [dA_t[0][0:8, :], dA_t[1][0:8, :], h_t[0][0:8, :], h_t[1][0:8, :]]
    rbf_bufs = ["dA_t0", "dA_t1", "h_t0", "h_t1"]
    dpr = [sp1[0:8, :], sp2[0:8, :], w_t[0][0:8, :], w_t[1][0:8, :]]
    dpr_bufs = ["sp1", "sp2", "w_t0", "w_t1"]
    proj_h = [dt_t[0], dt_t[1]]
    proj_bufs = ["dt_t0", "dt_t1"]
    lsc = sb_("lsc", [128, 4], F32)
    em_t = sb_("em_t", [128, 4], F32)
    mx_t = sb_("mx_t", [128, 2], F32)
    mx2_t = sb_("mx2_t", [128, 2], F32)
    sm_t = sb_("sm_t", [128, 2], F32)
    sm2_t = sb_("sm2_t", [128, 2], F32)
    msk_t = sb_("msk_t", [128, 4], F32)
    sub_t = sb_("sub_t", [128, 4], F32)
    rcp_t = sb_("rcp_t", [128, 2], F32)
    mx_c = mx_t[0:128, 0:1]
    sm_c = sm_t[0:128, 0:1]
    msk4 = msk_t[0:128, 0:4]
    tscal = sb_("tscal", [1, 2], F32)
    fjunk = sb_("fjunk", [1, 16], F32)
    fjunk2 = sb_("fjunk2", [1, 16], F32)
    tscal2 = sb_("tscal2", [1, 2], F32)
    eps_c = sb_("eps_c", [1, 1], F32)
    tapb32 = sb_("tapb32", [128, 512], F32) if DBG else None
    tapb16 = sb_("tapb16", [128, 512], F16) if DBG else None

    pb = [ctx.enter_context(nc.psum_tensor(f"pb{i}", [128, 512], F32))
          for i in range(8)]

    P = Plan()
    bufs = {}

    def B_(name):
        if name not in bufs:
            bufs[name] = Buf(name)
        return bufs[name]

    def dma(out, in_, reads=(), writes=()):
        P.op("Y", lambda e, out=out, in_=in_: e.dma_start(out=out, in_=in_),
             reads=reads, writes=writes, dma=True)

    def T(fn, reads=(), writes=()):
        P.op("T", fn, reads=reads, writes=writes)

    def V(fn, reads=(), writes=()):
        P.op("V", fn, reads=reads, writes=writes)

    def S(fn, reads=(), writes=()):
        P.op("S", fn, reads=reads, writes=writes)

    CVT32, CVT16 = B_("bc_B"), B_("cvt16")

    dma(wq_conv[:], conv_w2[:, :], writes=[B_("wq_conv")])
    dma(b_dt[:], dt_bias[:, :], writes=[B_("b_dt")])
    dma(b_conv[:], conv_bias[:, :], writes=[B_("b_conv")])
    dma(b_D[:], D_vec[:, :], writes=[B_("b_D")])
    dma(b_fg[:], fg_bias[:, :], writes=[B_("b_fg")])
    dma(b_bp[:], bp_bias[:, :], writes=[B_("b_bp")])
    dma(ident[:], ident_in[:, :], writes=[B_("ident")])
    dma(tscal[0:1, 0:1], temp_in[:, :], writes=[B_("tscal")])
    V(lambda e: e.tensor_copy(out=ident_h[:], in_=ident[:]),
      reads=[B_("ident")], writes=[B_("ident_h")])

    V(lambda e: e.memset(ones_c32[:], 1.0), writes=[B_("ones_c32")])
    V(lambda e: e.memset(eps_c[:], LN_EPS), writes=[B_("eps_c")])
    V(lambda e: e.memset(ones_r32[:], 1.0), writes=[B_("ones_r32")])
    for b in range(B_PER):
        V(lambda e, b=b: e.memset(states[b][:], 0.0), writes=[B_(f"states{b}")])
        V(lambda e, b=b: e.memset(hist[b][:].rearrange("p k c -> p (k c)"), 0.0),
          writes=[B_(f"hist{b}")])

    dma(cvt32[:, 0:KI * D_STATE], A_logt[:, :], reads=[CVT32], writes=[CVT32])
    S(lambda e: e.activation(out=A_pos[:], in_=cvt32[:, 0:KI * D_STATE],
                             func=AF.Exp),
      reads=[CVT32], writes=[B_("A_pos")])
    V(lambda e: e.tensor_scalar_mul(A_t[:], A_pos[:], -1.0),
      reads=[B_("A_pos")], writes=[B_("A_t")])

    def load16(dst_tile, dst_buf, src_ap, nk, c):
        cols = nk * c
        dma(cvt32[:, 0:cols].rearrange("p (k c) -> p k c", c=c), src_ap,
            reads=[CVT32], writes=[CVT32])
        P.op("V", lambda e: e.tensor_copy(out=dst_tile, in_=cvt32[:, 0:cols]),
             reads=[CVT32], writes=[dst_buf])

    load16(wq_xproj[:], B_("wq_xproj"),
           w_xproj.rearrange("(k p) c -> p k c", p=128), KI, 96)
    load16(wq_wp[:], B_("wq_wp"),
           wT_wp.rearrange("(k p) c -> p k c", p=128), KD, 256)
    load16(wq_wd[:], B_("wq_wd"),
           wT_wd.rearrange("(k p) c -> p k c", p=128), KD, 32)
    load16(sim_h[:], B_("sim_h"),
           sim_in.rearrange("(k p) c -> p k c", p=128), 2, 4)
    dma(cvt32[0:64, 0:KI * 128], w_dtproj[:, :], reads=[CVT32], writes=[CVT32])
    V(lambda e: e.tensor_copy(out=wq_dtproj[:], in_=cvt32[0:64, 0:KI * 128]),
      reads=[CVT32], writes=[B_("wq_dtproj")])
    for i in range(4):
        dma(cvt32[0:8, 0:D_MODEL], wT_wu[i], reads=[CVT32], writes=[CVT32])
        P.op("V", lambda e, i=i: e.tensor_copy(out=wq_wu[i][:],
                                               in_=cvt32[0:8, 0:D_MODEL]),
             reads=[CVT32], writes=[B_(f"wq_wu{i}")])

    dma(cvt32[:, 0:8].rearrange("p (k c) -> p k c", c=4),
        sim_in.rearrange("(k p) c -> p k c", p=128),
        reads=[CVT32], writes=[CVT32])
    S(lambda e: e.activation(out=cvt32[:, 0:8], in_=cvt32[:, 0:8],
                             func=AF.Square),
      reads=[CVT32], writes=[CVT32])
    for k in range(2):
        T(lambda e, k=k: e.matmul(out=pb[7][0:1, 0:4], lhsT=ones_c32[:],
                                  rhs=cvt32[:, 4 * k:4 * k + 4],
                                  start=(k == 0), stop=(k == 1)),
          reads=[CVT32, B_("ones_c32")], writes=[B_("pb7")])
    V(lambda e: e.tensor_scalar_max(row_t[0:1, 0:4], pb[7][0:1, 0:4], 1e-24),
      reads=[B_("pb7")], writes=[B_("row_t")])
    S(lambda e: e.activation(out=row_t[0:1, 0:4], in_=row_t[0:1, 0:4],
                             func=AF.Ln),
      reads=[B_("row_t")], writes=[B_("row_t")])
    S(lambda e: e.activation(out=row_t[0:1, 0:4], in_=row_t[0:1, 0:4],
                             func=AF.Exp, scale=-0.5),
      reads=[B_("row_t")], writes=[B_("row_t")])
    V(lambda e: e.tensor_scalar_min(tscal2[0:1, 0:1], tscal[0:1, 0:1],
                                    CLAMP_MAX),
      reads=[B_("tscal")], writes=[B_("tscal2")])
    S(lambda e: e.activation(out=tscal2[0:1, 0:1], in_=tscal2[0:1, 0:1],
                             func=AF.Exp),
      reads=[B_("tscal2")], writes=[B_("tscal2")])
    V(lambda e: e.tensor_tensor(out=row_t[0:1, 0:4], in0=row_t[0:1, 0:4],
                                in1=tscal2[0:1, 0:1].broadcast_to([1, 4]),
                                op=ALU.mult),
      reads=[B_("row_t"), B_("tscal2")], writes=[B_("row_t")])
    T(lambda e: e.transpose(out=pb[7][0:4, 0:1], in_=row_t[0:1, 0:4],
                            identity=ident[0:1, 0:1]),
      reads=[B_("row_t"), B_("ident")], writes=[B_("pb7")])
    S(lambda e: e.copy(out=sc4[:], in_=pb[7][0:4, 0:1]),
      reads=[B_("pb7")], writes=[B_("sc4")])

    def convert_w(src, dst, dstname, rows, cols):
        for r0 in range(0, rows, 128):
            for c0 in range(0, cols, 2048):
                cw = min(2048, cols - c0)
                dma(cvt32[:, 0:cw], src[r0:r0 + 128, c0:c0 + cw],
                    reads=[CVT32], writes=[CVT32])
                P.op("V", lambda e, cw=cw, c0=c0: e.tensor_copy(
                    out=cvt16[:, c0:c0 + cw], in_=cvt32[:, 0:cw]),
                    reads=[CVT32, CVT16], writes=[CVT16])
                dma(dst[r0:r0 + 128, c0:c0 + cw], cvt16[:, c0:c0 + cw],
                    reads=[CVT16], writes=[B_(dstname)])

    convert_w(w_inproj, s_inproj, "s_inproj", D_MODEL, 2 * D_INNER)
    convert_w(w_fg, s_fg, "s_fg", D_MODEL, D_MODEL)
    convert_w(wT_ffn, s_ffn, "s_ffn", D_MODEL, D_MODEL)
    convert_w(w_outproj, s_outproj, "s_outproj", D_INNER, D_MODEL)

    def ln_stats(m, sf, sfb):
        sq = sq_f[m % 2]
        sqb = B_(f"sq_f{m % 2}")
        P.op("S", lambda e, sf=sf, sq=sq: e.activation(out=sq[:], in_=sf[:],
                                                       func=AF.Square),
             reads=[sfb], writes=[sqb])
        P.op("T", lambda e, sf=sf, m=m: e.matmul(
            out=pb[6][0:1, :], lhsT=ones_c32[:], rhs=sf[:],
            start=(m == 0), stop=(m == KD - 1)),
            reads=[sfb, B_("ones_c32")], writes=[B_("pb6")])
        P.op("T", lambda e, sq=sq, m=m: e.matmul(
            out=pb[7][0:1, :], lhsT=ones_c32[:], rhs=sq[:],
            start=(m == 0), stop=(m == KD - 1)),
            reads=[sqb, B_("ones_c32")], writes=[B_("pb7")])

    def ln_rows_and_bcast():
        V(lambda e: e.tensor_scalar_mul(row_m[:], pb[6][0:1, :], 1.0 / D_MODEL),
          reads=[B_("pb6")], writes=[B_("row_m")])
        V(lambda e: e.tensor_scalar_mul(row_v[:], pb[7][0:1, :], 1.0 / D_MODEL),
          reads=[B_("pb7")], writes=[B_("row_v")])
        V(lambda e: e.tensor_mul(row_t[:], row_m[:], row_m[:]),
          reads=[B_("row_m")], writes=[B_("row_t")])
        V(lambda e: e.tensor_sub(row_v[:], row_v[:], row_t[:]),
          reads=[B_("row_v"), B_("row_t")], writes=[B_("row_v")])
        S(lambda e: e.activation(out=row_r[:], in_=row_v[:], func=AF.Ln,
                                 bias=eps_c[:]),
          reads=[B_("row_v"), B_("eps_c")], writes=[B_("row_r")])
        S(lambda e: e.activation(out=row_r[:], in_=row_r[:], func=AF.Exp,
                                 scale=-0.5),
          reads=[B_("row_r")], writes=[B_("row_r")])
        V(lambda e: e.tensor_mul(row_t[:], row_m[:], row_r[:]),
          reads=[B_("row_m"), B_("row_r")], writes=[B_("row_t")])
        T(lambda e: e.matmul(out=pb[6][:], lhsT=ones_r32[:], rhs=row_r[:],
                             start=True, stop=True),
          reads=[B_("row_r"), B_("ones_r32")], writes=[B_("pb6")])
        T(lambda e: e.matmul(out=pb[7][:], lhsT=ones_r32[:], rhs=row_t[:],
                             start=True, stop=True),
          reads=[B_("row_t"), B_("ones_r32")], writes=[B_("pb7")])

    def unit(b, tc):
        t0 = tc * TC
        SB = B_(f"states{b}")
        HB = B_(f"hist{b}")
        tap_on = DBG and (b, tc) == DBG_UNIT

        def tap16(slot, ap, rbuf):
            if tap_on:
                npart, w = ap.shape[0], ap.shape[-1]
                P.op("V", lambda e, ap=ap, npart=npart, w=w: e.tensor_copy(
                    out=tapb16[0:npart, 0:w], in_=ap),
                    reads=[rbuf, B_("tapb16")], writes=[B_("tapb16")],
                    label=f"tapcp16_{slot}")
                wd = max(w, 8)
                dma(dbg16[slot, 0:npart, 0:wd], tapb16[0:npart, 0:wd],
                    reads=[B_("tapb16")], writes=[B_(f"dbg16_{slot}")])

        def tap32(slot, ap, rbuf):
            if tap_on:
                npart, w = ap.shape[0], ap.shape[-1]
                P.op("V", lambda e, ap=ap, npart=npart, w=w: e.tensor_copy(
                    out=tapb32[0:npart, 0:w], in_=ap),
                    reads=[rbuf, B_("tapb32")], writes=[B_("tapb32")])
                wd = max(w, 8)
                dma(dbg32[slot, 0:npart, 0:wd], tapb32[0:npart, 0:wd],
                    reads=[B_("tapb32")], writes=[B_(f"dbg32_{slot}")])

        P.phase = 'P1' if (SERIAL_UNIT is None or (b, tc) == SERIAL_UNIT) else None
        for k in range(KD):
            xp, xpb = x32[k % 2], B_(f"x32_{k % 2}")
            dma(xp[:], xT[b, 128 * k:128 * (k + 1), t0:t0 + TC],
                reads=[xpb], writes=[xpb])
            P.op("V", lambda e, k=k, xp=xp: e.tensor_copy(out=x_h[:, k, :],
                                                          in_=xp[:]),
                 reads=[xpb], writes=[B_("x_h")],
                 fence_fn=(lambda e, k=k: e.tensor_copy(
                     out=fjunk2[:], in_=x_h[0:1, k, 0:16])) if k == KD - 1
                 else None)

        tap16(0, x_h[:, 0, :], B_("x_h"))
        P.phase = 'P2' if (SERIAL_UNIT is None or (b, tc) == SERIAL_UNIT) else None
        for piece in range(8):
            st, stb = stream[piece % 2], B_(f"stream{piece % 2}")
            stv = st[:].rearrange("p (k c) -> p k c", k=KD)
            dma(stv,
                s_inproj.rearrange("(k p) (j c) -> p k j c", p=128, c=512)
                [:, :, piece],
                reads=[stb], writes=[stb])
            for mj in range(4):
                m = piece * 4 + mj
                bank, bankb = pb[mj % 2], B_(f"pb{mj % 2}")
                for k in range(KD):
                    P.op("T", lambda e, stv=stv, mj=mj, k=k, bank=bank:
                         e.matmul(out=bank[:],
                                  lhsT=stv[:, k, 128 * mj:128 * mj + 128],
                                  rhs=x_h[:, k, :],
                                  start=(k == 0), stop=(k == KD - 1)),
                         reads=[stb, B_("x_h")], writes=[bankb])
                if m < 16:
                    P.op("S", lambda e, m=m, bank=bank: e.copy(
                        out=xi_pad[:, m, 3:515], in_=bank[:]),
                        reads=[bankb], writes=[B_("xi_pad")])
                else:
                    P.op("S", lambda e, m=m, bank=bank: e.activation(
                        out=zsil[:, m - 16, :], in_=bank[:], func=AF.Silu),
                        reads=[bankb], writes=[B_("zsil")])

        P.phase = 'P3' if (SERIAL_UNIT is None or (b, tc) == SERIAL_UNIT) else None
        V(lambda e: e.tensor_copy(out=xi_pad[:, :, 0:3], in_=hist[b][:]),
          reads=[HB, B_("xi_pad")], writes=[B_("xi_pad")])
        V(lambda e: e.tensor_copy(out=hist[b][:], in_=xi_pad[:, :, 512:515]),
          reads=[B_("xi_pad")], writes=[HB])
        for d in range(KI):
            acc, accb = q_t[d % 2], B_(f"q_t{d % 2}")
            P.op("V", lambda e, d=d, acc=acc: e.tensor_scalar(
                out=acc[:], in0=xi_pad[:, d, 0:512],
                scalar1=wq_conv[:, 4 * d:4 * d + 1], scalar2=None,
                op0=ALU.mult),
                reads=[B_("xi_pad"), B_("wq_conv")], writes=[accb])
            for j in range(1, 4):
                P.op("V", lambda e, d=d, j=j, acc=acc: e.scalar_tensor_tensor(
                    out=acc[:], in0=xi_pad[:, d, j:j + 512],
                    scalar=wq_conv[:, 4 * d + j:4 * d + j + 1], in1=acc[:],
                    op0=ALU.mult, op1=ALU.add),
                    reads=[B_("xi_pad"), B_("wq_conv"), accb], writes=[accb])
            P.op("S", lambda e, d=d, acc=acc: e.activation(
                out=u_all[:, d, :], in_=acc[:], func=AF.Silu,
                bias=b_conv[:, d:d + 1]),
                reads=[accb, B_("b_conv")], writes=[B_("u_all")],
                fence_fn=lambda e, d=d: e.copy(out=fjunk[:],
                                               in_=u_all[0:1, d, 0:16]))

        tap16(1, xi_pad[:, 0, 3:515], B_("xi_pad"))
        tap16(2, zsil[:, 0, :], B_("zsil"))
        tap16(3, u_all[:, 0, :], B_("u_all"))
        P.phase = 'P4' if (SERIAL_UNIT is None or (b, tc) == SERIAL_UNIT) else None
        uv = u_all[:].rearrange("p k c -> p k c")
        wxv = wq_xproj[:].rearrange("p (k c) -> p k c", c=96)
        for k in range(KI):
            P.op("T", lambda e, k=k: e.matmul(
                out=pb[2][0:96, :], lhsT=wxv[:, k, :], rhs=u_all[:, k, :],
                start=(k == 0), stop=(k == KI - 1)),
                reads=[B_("wq_xproj"), B_("u_all")], writes=[B_("pb2")])
        P.op("S", lambda e: e.copy(out=dtlow[:], in_=pb[2][0:64, :]),
             reads=[B_("pb2")], writes=[B_("dtlow")],
             fence_fn=lambda e: e.copy(out=fjunk[:], in_=dtlow[0:1, 0:16]))
        P.op("S", lambda e: e.copy(out=bcrows[:], in_=pb[2][64:96, :]),
             reads=[B_("pb2")], writes=[B_("bcrows")],
             fence_fn=lambda e: e.copy(out=fjunk[:], in_=bcrows[0:1, 0:16]))

        tap16(4, dtlow[:], B_("dtlow"))
        tap16(5, bcrows[:], B_("bcrows"))
        P.phase = 'P5' if (SERIAL_UNIT is None or (b, tc) == SERIAL_UNIT) else None
        dma(s_bcrows[:, :], bcrows[:], reads=[B_("bcrows")],
            writes=[B_("s_bcrows")])
        dma(bc_B[:],
            s_bcrows[0:16, :].unsqueeze(0).broadcast_to([128, 16, TC]),
            reads=[B_("s_bcrows"), B_("bc_B")], writes=[B_("bc_B")])
        dma(bc_C[:],
            s_bcrows[16:32, :].unsqueeze(0).broadcast_to([128, 16, TC]),
            reads=[B_("s_bcrows"), B_("bc_C")], writes=[B_("bc_C")])

        tap16(12, bc_B[:, 0, :], B_("bc_B"))
        tap16(13, bc_C[:, 0, :], B_("bc_C"))
        P.phase = 'P6' if (SERIAL_UNIT is None or (b, tc) == SERIAL_UNIT) else None
        wdv = wq_dtproj[:].rearrange("p (k c) -> p k c", c=128)
        for d in range(KI):
            par = d % 2
            dtt, wt = dt_t[par], w_t[par]
            dttb, wtb = B_(f"dt_t{par}"), B_(f"w_t{par}")
            P.op("T", lambda e, d=d: e.matmul(
                out=pb[2][:], lhsT=wdv[:, d, :], rhs=dtlow[:],
                start=True, stop=True),
                reads=[B_("wq_dtproj"), B_("dtlow")], writes=[B_("pb2")])
            P.op("S", lambda e, d=d: e.activation(
                out=sp1[:], in_=pb[2][:], func=AF.Relu, bias=b_dt[:, d:d + 1]),
                reads=[B_("pb2"), B_("b_dt")], writes=[B_("sp1")])
            P.op("S", lambda e, d=d: e.activation(
                out=sp2[:], in_=pb[2][:], func=AF.Abs, bias=b_dt[:, d:d + 1]),
                reads=[B_("pb2"), B_("b_dt")], writes=[B_("sp2")])
            S(lambda e: e.activation(out=sp2[:], in_=sp2[:], func=AF.Exp,
                                     scale=-1.0),
              reads=[B_("sp2")], writes=[B_("sp2")])
            S(lambda e: e.activation(out=sp2[:], in_=sp2[:], func=AF.Ln,
                                     bias=1.0),
              reads=[B_("sp2")], writes=[B_("sp2")])
            P.op("V", lambda e, dtt=dtt: e.tensor_add(dtt[:], sp1[:], sp2[:]),
                 reads=[B_("sp1"), B_("sp2")], writes=[dttb],
                 fence_fn=lambda e, dtt=dtt: e.tensor_copy(
                     out=fjunk2[:], in_=dtt[0:1, 0:16]))
            P.op("V", lambda e, d=d, dtt=dtt, wt=wt: e.tensor_mul(
                wt[:], dtt[:], u_all[:, d, :]),
                reads=[dttb, B_("u_all")], writes=[wtb])
            if d == 0:
                tap16(6, dtt[:], dttb)
                tap16(7, wt[:], wtb)
            ybank, ybankb = pb[3 + d % 2], B_(f"pb{3 + d % 2}")
            for n in range(D_STATE):
                pq = n % 2
                dAt, qt, ht, gt = dA_t[pq], q_t[pq], h_t[pq], g_t[pq]
                dAb, qb, hb, gb = (B_(f"dA_t{pq}"), B_(f"q_t{pq}"),
                                   B_(f"h_t{pq}"), B_(f"g_t{pq}"))
                col = d * D_STATE + n
                P.op("S", lambda e, dtt=dtt, dAt=dAt, col=col: e.activation(
                    out=dAt[:], in_=dtt[:], func=AF.Exp,
                    scale=A_t[:, col:col + 1]),
                    reads=[dttb, B_("A_t")], writes=[dAb])
                P.op("V", lambda e, wt=wt, qt=qt, n=n: e.tensor_mul(
                    qt[:], wt[:], bc_B[:, n, :]),
                    reads=[wtb, B_("bc_B")], writes=[qb])
                P.op("V", lambda e, dAt=dAt, qt=qt, ht=ht, col=col, b=b:
                     e.tensor_tensor_scan(
                         out=ht[:], data0=dAt[:], data1=qt[:],
                         initial=states[b][:, col:col + 1],
                         op0=ALU.mult, op1=ALU.add),
                     reads=[dAb, qb, SB], writes=[hb])
                P.op("V", lambda e, ht=ht, col=col, b=b: e.tensor_copy(
                    out=states[b][:, col:col + 1], in_=ht[:, 511:512]),
                    reads=[hb], writes=[SB])
                P.op("V", lambda e, ht=ht, gt=gt, n=n: e.tensor_mul(
                    gt[:], ht[:], bc_C[:, n, :]),
                    reads=[hb, B_("bc_C")], writes=[gb],
                    fence_fn=lambda e, gt=gt: e.tensor_copy(
                        out=fjunk2[:], in_=gt[0:1, 0:16]))
                if d == 0 and n == 0:
                    tap16(8, dAt[:], dAb)
                    tap16(9, qt[:], qb)
                    tap16(10, ht[:], hb)
                    tap16(11, gt[:], gb)
                P.op("T", lambda e, gt=gt, ybank=ybank, n=n: e.matmul(
                    out=ybank[:], lhsT=ident_h[:],
                    rhs=gt[:], start=(n == 0), stop=(n == D_STATE - 1)),
                    reads=[gb, B_("ident_h")], writes=[ybankb])
            yf, yfb = y_f[d % 2], B_(f"y_f{d % 2}")
            P.op("S", lambda e, ybank=ybank, yf=yf: e.copy(out=yf[:],
                                                           in_=ybank[:]),
                 reads=[ybankb], writes=[yfb])
            P.op("V", lambda e, d=d, yf=yf: e.scalar_tensor_tensor(
                out=yf[:], in0=u_all[:, d, :], scalar=b_D[:, d:d + 1],
                in1=yf[:], op0=ALU.mult, op1=ALU.add),
                reads=[B_("u_all"), B_("b_D"), yfb], writes=[yfb])
            P.op("V", lambda e, d=d, yf=yf: e.tensor_mul(
                y_g[:, d, :], yf[:], zsil[:, d, :]),
                reads=[yfb, B_("zsil")], writes=[B_("xi_pad")],
                fence_fn=lambda e, d=d: e.tensor_copy(
                    out=fjunk2[:], in_=y_g[0:1, d, 0:16]))
            if d == 0:
                tap16(18, yf[:], yfb)
                tap16(14, y_g[:, 0, :], B_("xi_pad"))

        P.phase = 'P7' if (SERIAL_UNIT is None or (b, tc) == SERIAL_UNIT) else None
        for piece in range(4):
            st, stb = stream[piece % 2], B_(f"stream{piece % 2}")
            stv = st[:].rearrange("p (k c) -> p k c", k=KI)
            dma(stv,
                s_outproj.rearrange("(k p) (j c) -> p k j c", p=128, c=256)
                [:, :, piece],
                reads=[stb], writes=[stb])
            for mj in range(2):
                m = piece * 2 + mj
                bank, bankb = pb[mj % 2], B_(f"pb{mj % 2}")
                for k in range(KI):
                    P.op("T", lambda e, stv=stv, mj=mj, k=k, bank=bank:
                         e.matmul(out=bank[:],
                                  lhsT=stv[:, k, 128 * mj:128 * mj + 128],
                                  rhs=y_g[:, k, :],
                                  start=(k == 0), stop=(k == KI - 1)),
                         reads=[stb, B_("xi_pad")], writes=[bankb])
                P.op("S", lambda e, m=m, bank=bank: e.copy(
                    out=new_all[:, 512 * m:512 * m + 512], in_=bank[:]),
                    reads=[bankb], writes=[CVT16])
                if m == 0:
                    tap16(15, new_all[:, 0:512], CVT16)
        for piece in range(2):
            fgs, fgsb = stream[piece % 2], B_(f"stream{piece % 2}")
            fgv = fgs[:].rearrange("p (k c) -> p k c", k=KD)
            dma(fgv,
                s_fg.rearrange("(k p) (j c) -> p k j c", p=128, c=512)
                [:, :, piece],
                reads=[fgsb], writes=[fgsb])
            for mj in range(4):
                m = piece * 4 + mj
                bank, bankb = pb[2 + m % 2], B_(f"pb{2 + m % 2}")
                for k in range(KD):
                    P.op("T", lambda e, fgv=fgv, mj=mj, k=k, bank=bank:
                         e.matmul(out=bank[:],
                                  lhsT=fgv[:, k, 128 * mj:128 * mj + 128],
                                  rhs=x_h[:, k, :],
                                  start=(k == 0), stop=(k == KD - 1)),
                         reads=[fgsb, B_("x_h")], writes=[bankb],
                         label=f"fgmm_m{m}_k{k}")
                fgt, fgtb = g_t[m % 2], B_(f"g_t{m % 2}")
                if tap_on and m == 0:
                    P.op("S", lambda e, bank=bank: e.copy(out=sq_f[1][:],
                                                          in_=bank[:]),
                         reads=[bankb], writes=[B_("sq_f1")])
                    tap32(6, sq_f[1][:], B_("sq_f1"))
                P.op("S", lambda e, m=m, bank=bank, fgt=fgt: e.activation(
                    out=fgt[:], in_=bank[:], func=AF.Sigmoid,
                    bias=b_fg[:, m:m + 1]),
                    reads=[bankb, B_("b_fg")], writes=[fgtb],
                    label=f"sigmoid_m{m}")
                xp, xpb = x32[m % 2], B_(f"x32_{m % 2}")
                dma(xp[:], xT[b, 128 * m:128 * (m + 1), t0:t0 + TC],
                    reads=[xpb], writes=[xpb])
                sf, sfb = s_f[m % 2], B_(f"s_f{m % 2}")
                nv = new_all[:, 512 * m:512 * m + 512]
                P.op("V", lambda e, xp=xp, nv=nv, sf=sf: e.tensor_sub(
                    sf[:], xp[:], nv),
                    reads=[xpb, CVT16], writes=[sfb])
                P.op("V", lambda e, sf=sf, fgt=fgt: e.tensor_mul(sf[:], sf[:],
                                                                 fgt[:]),
                     reads=[sfb, fgtb], writes=[sfb])
                P.op("V", lambda e, sf=sf, nv=nv: e.tensor_add(sf[:], sf[:],
                                                               nv),
                     reads=[sfb, CVT16], writes=[sfb])
                P.op("V", lambda e, sf=sf, xp=xp: e.tensor_add(sf[:], sf[:],
                                                               xp[:]),
                     reads=[sfb, xpb], writes=[sfb])
                if m == 0:
                    tap16(16, fgt[:], fgtb)
                    tap32(1, sf[:], sfb)
                ln_stats(m, sf, sfb)
                P.op("V", lambda e, m=m, sf=sf: e.tensor_copy(
                    out=x2_32[:, m, :], in_=sf[:]),
                    reads=[sfb], writes=[B_("x2_32")])
        ln_rows_and_bcast()
        for m in range(KD):
            P.op("V", lambda e, m=m: e.tensor_mul(x2_32[:, m, :],
                                                  x2_32[:, m, :], pb[6][:]),
                 reads=[B_("x2_32"), B_("pb6")], writes=[B_("x2_32")])
            P.op("V", lambda e, m=m: e.tensor_sub(x2_32[:, m, :],
                                                  x2_32[:, m, :], pb[7][:]),
                 reads=[B_("x2_32"), B_("pb7")], writes=[B_("x2_32")])
            P.op("V", lambda e, m=m: e.tensor_copy(out=x2_h[:, m, :],
                                                   in_=x2_32[:, m, :]),
                 reads=[B_("x2_32")], writes=[B_("x2_h")],
                 fence_fn=(lambda e, m=m: e.tensor_copy(
                     out=fjunk2[:], in_=x2_h[0:1, m, 0:16])) if m == KD - 1
                 else None)
        tap32(2, x2_32[:, 0, :], B_("x2_32"))
        tap16(17, x2_h[:, 0, :], B_("x2_h"))

        P.phase = 'P8' if (SERIAL_UNIT is None or (b, tc) == SERIAL_UNIT) else None
        wpv = wq_wp[:].rearrange("p (k c) -> p k c", c=256)
        for mp in range(2):
            bank, bankb = pb[2 + mp], B_(f"pb{2 + mp}")
            for k in range(KD):
                P.op("T", lambda e, mp=mp, k=k, bank=bank: e.matmul(
                    out=bank[:], lhsT=wpv[:, k, 128 * mp:128 * mp + 128],
                    rhs=x2_h[:, k, :], start=(k == 0), stop=(k == KD - 1)),
                    reads=[B_("wq_wp"), B_("x2_h")], writes=[bankb])
            P.op("S", lambda e, mp=mp, bank=bank: e.add(
                out=proj_h[mp][:], in_=bank[:], add=b_bp[:, mp:mp + 1]),
                reads=[bankb, B_("b_bp")], writes=[B_(proj_bufs[mp])],
                fence_fn=lambda e, mp=mp: e.copy(out=fjunk[:],
                                                 in_=proj_h[mp][0:1, 0:16]))
            sq, sqb = sq_f[mp], B_(f"sq_f{mp}")
            P.op("S", lambda e, mp=mp, bank=bank, sq=sq: e.activation(
                out=sq[:], in_=bank[:], func=AF.Square,
                bias=b_bp[:, mp:mp + 1]),
                reads=[bankb, B_("b_bp")], writes=[sqb])
            P.op("T", lambda e, sq=sq, mp=mp: e.matmul(
                out=pb[4][0:1, :], lhsT=ones_c32[:], rhs=sq[:],
                start=(mp == 0), stop=(mp == 1)),
                reads=[sqb, B_("ones_c32")], writes=[B_("pb4")])
        V(lambda e: e.tensor_scalar_max(row_t[:], pb[4][0:1, :], 1e-24),
          reads=[B_("pb4")], writes=[B_("row_t")])
        S(lambda e: e.activation(out=row_r[:], in_=row_t[:], func=AF.Ln),
          reads=[B_("row_t")], writes=[B_("row_r")])
        S(lambda e: e.activation(out=row_r[:], in_=row_r[:], func=AF.Exp,
                                 scale=-0.5),
          reads=[B_("row_r")], writes=[B_("row_r")])
        tap32(5, row_r[:], B_("row_r"))
        for k in range(2):
            P.op("T", lambda e, k=k: e.matmul(
                out=pb[5][0:4, :],
                lhsT=sim_h[:].rearrange("p (k c) -> p k c", c=4)[:, k, :],
                rhs=proj_h[k][:], start=(k == 0), stop=(k == 1)),
                reads=[B_("sim_h"), B_(proj_bufs[k])], writes=[B_("pb5")])
        P.op("V", lambda e: e.tensor_scalar(out=lt4[:], in0=pb[5][0:4, :],
                                            scalar1=sc4[:], scalar2=None,
                                            op0=ALU.mult),
             reads=[B_("pb5"), B_("sc4")], writes=[B_("sq_f0")],
             fence_fn=lambda e: e.tensor_copy(out=fjunk2[:],
                                              in_=lt4[0:1, 0:16]))
        for blk in range(4):
            P.op("T", lambda e, blk=blk: e.transpose(
                out=pb[4][:, 5 * blk:5 * blk + 4],
                in_=lt4[:, 128 * blk:128 * blk + 128],
                identity=ident[0:4, 0:4]),
                reads=[B_("sq_f0"), B_("ident")], writes=[B_("pb4")])
            P.op("T", lambda e, blk=blk: e.transpose(
                out=pb[4][:, 5 * blk + 4:5 * blk + 5],
                in_=row_r[:, 128 * blk:128 * blk + 128],
                identity=ident[0:1, 0:1]),
                reads=[B_("row_r"), B_("ident")], writes=[B_("pb4")])
        S(lambda e: e.copy(out=ltt[:], in_=pb[4][:, 0:20]),
          reads=[B_("pb4")], writes=[B_("ltt")])
        for blk in range(4):
            lv = ltt[:, 5 * blk:5 * blk + 4]
            rv = ltt[:, 5 * blk + 4:5 * blk + 5]
            P.op("V", lambda e, lv=lv, rv=rv: e.tensor_tensor(
                out=lsc[:], in0=lv, in1=rv.broadcast_to([128, 4]),
                op=ALU.mult),
                   reads=[B_("ltt")], writes=[B_("lsc")],
                label=f"lscw_b{blk}", sync_self=True)
            P.op("V", lambda e: e.tensor_tensor(out=mx2_t[:], in0=lsc[:, 0:2],
                                        in1=lsc[:, 2:4], op=ALU.max),
                 reads=[B_("lsc")], writes=[B_("mx2_t")], sync_self=True)
            P.op("V", lambda e: e.tensor_tensor(out=mx_c, in0=mx2_t[0:128, 0:1],
                                        in1=mx2_t[0:128, 1:2], op=ALU.max),
                 reads=[B_("mx2_t")], writes=[B_("mx_t")], sync_self=True)
            P.op("V", lambda e: e.tensor_tensor(out=sub_t[:], in0=lsc[:],
                                        in1=mx_c.broadcast_to([128, 4]),
                                        op=ALU.subtract),
                 reads=[B_("lsc"), B_("mx_t")], writes=[B_("sub_t")], sync_self=True)
            P.op("S", lambda e: e.activation(out=em_t[:], in_=sub_t[:],
                                             func=AF.Exp),
                    reads=[B_("sub_t")], writes=[B_("em_t")],
                 label=f"smexp_b{blk}", sync_self=True)
            P.op("V", lambda e: e.tensor_tensor(out=sm2_t[:], in0=em_t[:, 0:2],
                                        in1=em_t[:, 2:4], op=ALU.add),
                 reads=[B_("em_t")], writes=[B_("sm2_t")], sync_self=True)
            P.op("V", lambda e: e.tensor_tensor(out=sm_c, in0=sm2_t[0:128, 0:1],
                                        in1=sm2_t[0:128, 1:2], op=ALU.add),
                 reads=[B_("sm2_t")], writes=[B_("sm_t")], sync_self=True)
            P.op("V", lambda e: e.tensor_tensor(out=msk4, in0=lsc[:],
                                        in1=mx_c.broadcast_to([128, 4]),
                                        op=ALU.is_ge),
                 reads=[B_("lsc"), B_("mx_t")], writes=[B_("msk_t")], sync_self=True)
            P.op("V", lambda e: e.tensor_mul(sub_t[:], em_t[:], msk4),
                    reads=[B_("em_t"), B_("msk_t")], writes=[B_("sub_t")],
                 label=f"smmul_b{blk}", sync_self=True)
            P.op("V", lambda e: e.reciprocal(rcp_t[0:128, 0:1], sm_c),
                    reads=[B_("sm_t")], writes=[B_("rcp_t")], slow_write=True, sync_self=True)
            P.op("V", lambda e, blk=blk: e.tensor_tensor(
                out=retc[:, 4 * blk:4 * blk + 4], in0=sub_t[:],
                in1=rcp_t[0:128, 0:1].broadcast_to([128, 4]), op=ALU.mult),
                   reads=[B_("sub_t"), B_("rcp_t")], writes=[B_("retc")],
                label=f"retw_b{blk}", sync_self=True)
        for blk in range(4):
            P.op("T", lambda e, blk=blk: e.transpose(
                out=pb[5][0:4, 128 * blk:128 * blk + 128],
                in_=retc[:, 4 * blk:4 * blk + 4],
                identity=ident[:]),
                reads=[B_("retc"), B_("ident")], writes=[B_("pb5")])
        P.op("S", lambda e: e.copy(out=retf[:], in_=pb[5][0:4, :]),
             reads=[B_("pb5")], writes=[B_("retf")],
             fence_fn=lambda e: e.copy(out=fjunk[:], in_=retf[0:1, 0:16]))
        tap32(3, ltt[:], B_("ltt"))
        tap32(4, retc[:], B_("retc"))
        tap16(19, retf[:], B_("retf"))
        tap16(20, proj_h[0][:], B_(proj_bufs[0]))
        dma(s_retf[:, :], retf[:], reads=[B_("retf")], writes=[B_("s_retf")])
        for i in range(4):
            P.op("Y", lambda e, i=i: e.dma_start(
                out=rbf[i], in_=s_retf[i:i + 1, :].broadcast_to([8, TC])),
                reads=[B_("s_retf")], writes=[B_(rbf_bufs[i])],
                dma=True)
        wdvv = wq_wd[:].rearrange("p (k c) -> p k c", c=32)
        for i in range(4):
            bank, bankb = pb[i], B_(f"pb{i}")
            for k in range(KD):
                P.op("T", lambda e, i=i, k=k, bank=bank: e.matmul(
                    out=bank[0:8, :], lhsT=wdvv[:, k, 8 * i:8 * i + 8],
                    rhs=x2_h[:, k, :], start=(k == 0), stop=(k == KD - 1)),
                    reads=[B_("wq_wd"), B_("x2_h")], writes=[bankb])
            P.op("V", lambda e, i=i, bank=bank: e.tensor_mul(
                dpr[i], bank[0:8, :], rbf[i]),
                reads=[bankb, B_(rbf_bufs[i])], writes=[B_(dpr_bufs[i])],
                fence_fn=lambda e, i=i: e.tensor_copy(out=fjunk2[:],
                                                      in_=dpr[i][0:1, 0:16]))
        for piece in range(2):
            st, stb = stream[piece % 2], B_(f"stream{piece % 2}")
            stv = st[:].rearrange("p (k c) -> p k c", k=KD)
            dma(stv,
                s_ffn.rearrange("(k p) (j c) -> p k j c", p=128, c=512)
                [:, :, piece],
                reads=[stb], writes=[stb])
            for mj in range(4):
                m = piece * 4 + mj
                bank, bankb = pb[4 + m % 2], B_(f"pb{4 + m % 2}")
                for k in range(KD):
                    P.op("T", lambda e, stv=stv, mj=mj, k=k, bank=bank:
                         e.matmul(out=bank[:],
                                  lhsT=stv[:, k, 128 * mj:128 * mj + 128],
                                  rhs=x2_h[:, k, :],
                                  start=(k == 0), stop=False),
                         reads=[stb, B_("x2_h")], writes=[bankb])
                for i in range(4):
                    P.op("T", lambda e, i=i, m=m, bank=bank: e.matmul(
                        out=bank[:], lhsT=wq_wu[i][:, 128 * m:128 * m + 128],
                        rhs=dpr[i], start=False, stop=(i == 3)),
                        reads=[B_(f"wq_wu{i}"), B_(dpr_bufs[i])], writes=[bankb])
                sf, sfb = s_f[m % 2], B_(f"s_f{m % 2}")
                P.op("V", lambda e, m=m, bank=bank, sf=sf: e.tensor_add(
                    sf[:], x2_32[:, m, :], bank[:]),
                    reads=[B_("x2_32"), bankb], writes=[sfb])
                ln_stats(m, sf, sfb)
                P.op("V", lambda e, m=m, sf=sf: e.tensor_copy(
                    out=x2_32[:, m, :], in_=sf[:]),
                    reads=[sfb], writes=[B_("x2_32")])
        ln_rows_and_bcast()
        for m in range(KD):
            x3, x3b = x3_f[m % 2], B_(f"s_f{m % 2}")
            P.op("V", lambda e, m=m, x3=x3: e.tensor_mul(
                x3[:], x2_32[:, m, :], pb[6][:]),
                reads=[B_("x2_32"), B_("pb6")], writes=[x3b])
            P.op("V", lambda e, x3=x3: e.tensor_sub(x3[:], x3[:], pb[7][:]),
                 reads=[B_("pb7"), x3b], writes=[x3b])
            P.op("Y", lambda e, m=m, x3=x3, b=b, t0=t0: e.dma_start(
                out=outT[b, 128 * m:128 * (m + 1), t0:t0 + TC], in_=x3[:]),
                reads=[x3b], writes=[B_(f"outT_{b}_{tc}_{m}")], dma=True)

    units = []
    for b in range(B_PER):
        for tc in range(N_TC):
            units.append((b, tc))
    if n_units is not None:
        units = units[:n_units]
    for b, tc in units:
        unit(b, tc)



    def replay(ename):
        def body(eng):
            for fn, waits, inc, label, lane in P.ops[ename]:
                for e2, v in waits.items():
                    eng.wait_ge(sems[e2], v)
                instr = fn(eng)
                instr.then_inc(sems[lane], inc)
            if ename == "Y":
                for lane in DMA_LANES:
                    if P.cnt[lane]:
                        eng.wait_ge(sems[lane], P.cnt[lane])
        return body

    with nc.Block() as block:
        block.tensor(replay("T"))
        block.vector(replay("V"))
        block.scalar(replay("S"))
        block.sync(replay("Y"))

    ctx.close()
    return nc


def _prep_inputs(inp):
    x = np.ascontiguousarray(inp["L0_attn_x"])
    xT = np.ascontiguousarray(x.transpose(0, 2, 1))
    ident = np.eye(128, dtype=np.float32)

    wd_pad = np.zeros((D_MODEL, 32), np.float32)
    wu_pad = np.zeros((4, 8, D_MODEL), np.float32)
    ranks = [1, 2, 4, 8]
    for i, r in enumerate(ranks):
        wd = inp[f"L0_ffn_Wd{i}"]
        wu = inp[f"L0_ffn_Wu{i}"]
        wd_pad[:, 8 * i:8 * i + r] = np.ascontiguousarray(wd.T)
        wu_pad[i, 0:r, :] = np.ascontiguousarray(wu.T)

    common = dict(
        w_inproj=np.ascontiguousarray(inp["L0_attn_in_proj_w"]),
        w_xproj=np.ascontiguousarray(inp["L0_attn_x_proj_w"]),
        w_dtproj=np.ascontiguousarray(inp["L0_attn_dt_proj_w"]),
        w_outproj=np.ascontiguousarray(inp["L0_attn_out_proj_w"]),
        w_fg=np.ascontiguousarray(inp["L0_attn_fg_w"]),
        wT_ffn=np.ascontiguousarray(inp["L0_ffn_W"].T),
        wT_wp=np.ascontiguousarray(inp["L0_ffn_Wp"].T),
        wT_wd=wd_pad,
        wT_wu=wu_pad,
        conv_w2=np.ascontiguousarray(
            inp["L0_attn_conv_w"].reshape(KI, 128, D_CONV)
            .transpose(1, 0, 2).reshape(128, KI * D_CONV)),
        A_logt=np.ascontiguousarray(
            inp["L0_attn_A_log"].reshape(KI, 128, D_STATE)
            .transpose(1, 0, 2).reshape(128, KI * D_STATE)),
        dt_bias=np.ascontiguousarray(
            inp["L0_attn_dt_proj_b"].reshape(KI, 128).T),
        conv_bias=np.ascontiguousarray(
            inp["L0_attn_conv_b"].reshape(KI, 128).T),
        D_vec=np.ascontiguousarray(inp["L0_attn_D"].reshape(KI, 128).T),
        fg_bias=np.ascontiguousarray(inp["L0_attn_fg_b"].reshape(KD, 128).T),
        bp_bias=np.ascontiguousarray(inp["L0_ffn_bp"].reshape(2, 128).T),
        sim_in=np.ascontiguousarray(inp["L0_ffn_sim"]),
        temp_in=np.ascontiguousarray(inp["L0_ffn_temp"].reshape(1, 1)),
        ident_in=ident,
    )
    in_maps = []
    for c in range(N_CORES):
        m = dict(common)
        m["xT"] = np.ascontiguousarray(xT[c * B_PER:(c + 1) * B_PER])
        in_maps.append(m)
    return in_maps


def kernel(**inputs):
    in_maps = _prep_inputs(inputs)
    nc = build_nc()
    res = run_bass_kernel_spmd(nc, in_maps, core_ids=list(range(N_CORES)))
    outs = []
    for c in range(N_CORES):
        o = res.results[c]["outT"]
        outs.append(np.ascontiguousarray(o.transpose(0, 2, 1)))
    return np.concatenate(outs, axis=0).astype(np.float32)
